# version 13
# speedup vs baseline: 1.6077x; 1.0381x over previous
import numpy as np

import concourse.bass as bass
import concourse.mybir as mybir
from concourse.bass_utils import run_bass_kernel_spmd

F32 = mybir.dt.float32
F16 = mybir.dt.float16
BF16 = mybir.dt.bfloat16
AF = mybir.ActivationFunctionType
ALU = mybir.AluOpType
AX = mybir.AxisListType

NCORES = 8
SEQ_PC = 2
S = 2048
D = 1024
NH = 16
DH = 64
INNER = 4096
TPS = S // 128
NTT = SEQ_PC * TPS
QB = 512
NQB = S // QB
KCH = D // 128
RM = float(1.5 * 2 ** 23)
C0 = 4.0
C1 = 12.0
EPS = 1e-5

ENGS = ("tensor", "vector", "scalar", "gpsimd", "sync")
DBG = {"point": None, "skip": 0}


class Buf:
    __slots__ = ("name", "writers", "readers")

    def __init__(self, name):
        self.name = name
        self.writers = []
        self.readers = []

    @property
    def writer(self):
        return self.writers[-1] if self.writers else None


class Op:
    __slots__ = ("eng", "call", "sig", "sigval", "deps", "isdma", "small", "annot",
                 "lane", "prewait")

    def __init__(self, eng, call, annot=""):
        self.eng = eng
        self.call = call
        self.sig = False
        self.sigval = 0
        self.deps = []
        self.isdma = False
        self.small = False
        self.lane = None
        self.prewait = 0
        self.annot = annot


NDMALANES = 8


class Plan:
    def __init__(self):
        self.ops = {e: [] for e in ENGS}
        self.dma_rr = {"sync": 0, "gpsimd": 0}
        self.dma_lane_count = {("sync", i): 0 for i in range(NDMALANES)}
        self.dma_lane_count.update({("gpsimd", i): 0 for i in range(NDMALANES)})

    def emit(self, eng, call, reads=(), writes=(), sreads=(), isdma=False,
             small=False, partial=False, annot=""):
        op = Op(eng, call, annot)
        op.isdma = isdma
        op.small = small
        if isdma:
            li = self.dma_rr[eng]
            self.dma_rr[eng] = (li + 1) % NDMALANES
            self.dma_lane_count[(eng, li)] += 1
            op.lane = (eng, li)
            op.prewait = 16 * (self.dma_lane_count[(eng, li)] - 1)
            op.sigval = 16 * self.dma_lane_count[(eng, li)]
        self.ops[eng].append(op)
        for b in sreads:
            if (b.writer is not None and b.writer.eng == eng and not b.writer.isdma
                    and not b.writer.small):
                raise RuntimeError(
                    f"scalar-operand hazard on {b.name}: produced+consumed on {eng} ({annot})")
        for b in tuple(reads) + tuple(sreads):
            for w in b.writers:
                if w is not op and (w.eng != eng or w.isdma or w.small):
                    op.deps.append(w)
            b.readers.append(op)
        for b in writes:
            for w in b.writers:
                if w is not op and (w.eng != eng or w.isdma):
                    op.deps.append(w)
            for r in b.readers:
                if r is not op and (r.eng != eng or r.isdma):
                    op.deps.append(r)
            if partial:
                b.writers.append(op)
            else:
                b.writers = [op]
                b.readers = []
        return op

    def finalize(self):
        for e in ENGS:
            for op in self.ops[e]:
                for d in op.deps:
                    d.sig = True
                if op.isdma:
                    op.sig = True
        for e in ENGS:
            c = 0
            for op in self.ops[e]:
                if op.isdma:
                    continue
                if op.sig:
                    c += 1
                op.sigval = c

    def totals(self):
        return dict(self.dma_lane_count)

    def replay_one(self, eng_obj, ename, sems):
        waited = {}
        for op in self.ops[ename]:
            waits = {}
            for d in op.deps:
                if d.eng == ename and not d.isdma and not d.small:
                    continue
                key = d.lane if d.isdma else d.eng
                if d.sigval > waited.get(key, 0):
                    waits[key] = max(waits.get(key, 0), d.sigval)
            if op.isdma and op.prewait > waited.get(op.lane, 0):
                waits[op.lane] = max(waits.get(op.lane, 0), op.prewait)
            for pe in sorted(waits, key=str):
                if waits[pe] > 0:
                    eng_obj.wait_ge(sems[pe], waits[pe])
                    waited[pe] = waits[pe]
            inst = op.call(eng_obj)
            if op.sig:
                if inst is None:
                    inst = eng_obj.nop()
                inst.then_inc(sems[op.lane] if op.isdma else sems[ename],
                              16 if op.isdma else 1)



def host_constants():
    c = {}
    c["ident"] = np.eye(128, dtype=np.float32)
    c["tri01"] = np.triu(np.ones((128, 128), np.float32))
    t = np.arange(S, dtype=np.float32)
    inv_freq = 1.0 / (10000.0 ** (np.arange(0, DH, 2, dtype=np.float32) / DH))
    freqs = np.repeat(t[:, None] * inv_freq[None, :], 2, axis=-1)
    cos = np.cos(freqs).astype(np.float32)
    sin = np.sin(freqs).astype(np.float32)
    base = (np.arange(0, DH, 2, dtype=np.float32) + 0.4 * DH) / (1.4 * DH)
    power = (t - S // 2) / 512.0
    scale = (base[None, :] ** power[:, None]).astype(np.float32)
    scale = np.concatenate((scale, scale), -1)
    sgn = np.tile(np.array([-1.0, 1.0], np.float32), DH // 2)

    def tileize(m):
        return np.ascontiguousarray(
            m.reshape(TPS, 128, DH).transpose(1, 0, 2).reshape(128, TPS * DH))

    c["rot_csq"] = tileize(cos * scale)
    c["rot_ssq"] = tileize(sin * scale * sgn[None, :])
    c["rot_csk"] = tileize(cos / scale)
    c["rot_ssk"] = tileize(sin / scale * sgn[None, :])
    return c


def host_layouts(I):
    d = {}

    def T(w):
        return np.ascontiguousarray(np.asarray(w, np.float32).T)

    d["a0_wqT"] = T(I["L0_attn_Wq"]); a0bq = np.asarray(I["L0_attn_bq"], np.float32)
    d["a0_wkT"] = T(I["L0_attn_Wv"]); a0bk = np.asarray(I["L0_attn_bv"], np.float32)
    d["a0_wvT"] = T(I["L0_attn_Wk"]); a0bv = np.asarray(I["L0_attn_bk"], np.float32)
    d["a0_woT"] = T(I["L0_attn_Wo"]); a0bo = np.asarray(I["L0_attn_bo"], np.float32)
    d["a1_wqT"] = T(I["L1_attn_Wq"]); a1bq = np.asarray(I["L1_attn_bq"], np.float32)
    d["a1_wkT"] = T(I["L1_attn_Wk"]); a1bk = np.asarray(I["L1_attn_bk"], np.float32)
    d["a1_wvT"] = T(I["L1_attn_Wv"]); a1bv = np.asarray(I["L1_attn_bv"], np.float32)
    d["a1_woT"] = T(I["L1_attn_Wo"]); a1bo = np.asarray(I["L1_attn_bo"], np.float32)
    d["f0_wrT"] = T(I["L0_ffn_W_r"])
    d["f0_wkT"] = T(I["L0_ffn_W_k"])
    d["f0_wvT"] = T(I["L0_ffn_W_v"])
    d["f1_wgT"] = T(I["L1_ffn_w_gate"])
    d["f1_wuT"] = T(I["L1_ffn_w_up"])
    d["f1_wdT"] = T(I["L1_ffn_w_down"])
    rb = np.asarray(I["L1_attn_rel_bias"], np.float32)
    p = np.arange(128)[:, None]
    jj = np.arange(384)[None, :]
    band = rb[:, np.clip(jj - p, 0, 256)]
    d["band"] = np.ascontiguousarray(band.transpose(1, 0, 2).reshape(128, 16 * 384))
    eb = np.empty((32,), np.float32)
    eb[0::2] = rb[:, 0]
    eb[1::2] = rb[:, 256]
    d["expb_raw"] = np.ascontiguousarray(np.broadcast_to(eb, (128, 32)))

    def fm(v):
        return np.ascontiguousarray(np.asarray(v, np.float32).reshape(KCH, 128).T)

    d["a1_bq_fm"] = fm(a1bq)
    d["a1_bk_fm"] = fm(a1bk)
    d["mu_r_fm"] = fm(I["L0_ffn_mu_r"])
    d["mu_k_fm"] = fm(I["L0_ffn_mu_k"])

    def rep(v):
        return np.ascontiguousarray(np.broadcast_to(np.asarray(v, np.float32), (128, D)))

    d["a0_bq_rep"] = rep(a0bq); d["a0_bk_rep"] = rep(a0bk)
    d["a0_bv_rep"] = rep(a0bv); d["a0_bo_rep"] = rep(a0bo)
    d["a1_bv_rep"] = rep(a1bv); d["a1_bo_rep"] = rep(a1bo)
    return d


HOST_CONST_NAMES = ("ident", "tri01", "rot_csq", "rot_ssq", "rot_csk", "rot_ssk")
HOST_LAYOUT_2D = {
    "a0_wqT": [D, D], "a0_wkT": [D, D], "a0_wvT": [D, D], "a0_woT": [D, D],
    "a1_wqT": [D, D], "a1_wkT": [D, D], "a1_wvT": [D, D], "a1_woT": [D, D],
    "f0_wrT": [D, D], "f0_wkT": [D, D], "f0_wvT": [D, D],
    "f1_wgT": [D, INNER], "f1_wuT": [D, INNER], "f1_wdT": [INNER, D],
    "band": [128, 16 * 384], "expb_raw": [128, 32],
    "a1_bq_fm": [128, 8], "a1_bk_fm": [128, 8],
    "mu_r_fm": [128, 8], "mu_k_fm": [128, 8],
    "a0_bq_rep": [128, D], "a0_bk_rep": [128, D], "a0_bv_rep": [128, D],
    "a0_bo_rep": [128, D], "a1_bv_rep": [128, D], "a1_bo_rep": [128, D],
}

LNB = (0, 72, 144)


def build_program(upto=4):
    nc = bass.Bass("TRN2", target_bir_lowering=False)
    P = Plan()
    ctx = []

    def enter(cm):
        h = cm.__enter__()
        ctx.append(cm)
        return h

    hb = {}
    hb["x_in"] = nc.dram_tensor("x_in", [SEQ_PC * S, D], F32, kind="ExternalInput")
    hb["out"] = nc.dram_tensor("out", [SEQ_PC * S, D], F32, kind="ExternalOutput")
    for n in HOST_CONST_NAMES:
        shp = [128, 128] if n in ("ident", "tri01") else [128, TPS * DH]
        hb[n] = nc.dram_tensor(n, shp, F32, kind="ExternalInput")
    for n, shp in HOST_LAYOUT_2D.items():
        hb[n] = nc.dram_tensor(n, shp, F32, kind="ExternalInput")
    hb["wgq"] = nc.dram_tensor("wgq", [D, INNER], BF16)
    hb["wuq"] = nc.dram_tensor("wuq", [D, INNER], BF16)
    hb["wdq"] = nc.dram_tensor("wdq", [INNER, D], BF16)

    XRES = enter(nc.sbuf_tensor("XRES", [128, NTT * D], F16))
    KT = enter(nc.sbuf_tensor("KT", [128, KCH * S], BF16))
    VH = enter(nc.sbuf_tensor("VH", [128, TPS * NH * 65], BF16))
    XNT = enter(nc.sbuf_tensor("XNT", [128, KCH * QB], BF16))
    QT = enter(nc.sbuf_tensor("QT", [128, KCH * QB], BF16))
    INT = enter(nc.sbuf_tensor("INT", [128, KCH * QB], BF16))
    PT = enter(nc.sbuf_tensor("PT", [128, 4 * QB], BF16))
    WA = enter(nc.sbuf_tensor("WA", [128, KCH * D], BF16))
    BREP = enter(nc.sbuf_tensor("BREP", [128, D], BF16))
    BREP2 = enter(nc.sbuf_tensor("BREP2", [128, D], BF16))
    ROT = enter(nc.sbuf_tensor("ROT", [128, 4 * TPS * DH], BF16))
    IDENT = enter(nc.sbuf_tensor("IDENT", [128, 128], BF16))
    TRI01 = enter(nc.sbuf_tensor("TRI01", [128, 128], BF16))
    EXPB = enter(nc.sbuf_tensor("EXPB", [128, 38], F32))
    BFM = enter(nc.sbuf_tensor("BFM", [128, 48], F32))
    ONES1 = enter(nc.sbuf_tensor("ONES1", [1, 128], BF16))
    SC1 = enter(nc.sbuf_tensor("SC1", [128, 4 * D], BF16))
    SC2 = enter(nc.sbuf_tensor("SC2", [128, D], F32))
    SC3 = enter(nc.sbuf_tensor("SC3", [128, D], F32))
    QSTG = enter(nc.sbuf_tensor("QSTG", [128, 1024], F32))
    SML = enter(nc.sbuf_tensor("SML", [128, 256], F32))
    SWS = enter(nc.sbuf_tensor("SWS", [128, 16], F32))

    PS1 = enter(nc.psum_tensor("PS1", [128, 2048], F32))
    PS2 = enter(nc.psum_tensor("PS2", [128, 1024], F32))
    PS3 = enter(nc.psum_tensor("PS3", [128, 512], F32))
    PST = enter(nc.psum_tensor("PST", [128, 1024], BF16))

    sems = {e: enter(nc.semaphore(f"sem_{e}")) for e in ("tensor", "vector", "scalar")}
    for q in ("sync", "gpsimd"):
        sems[q] = enter(nc.semaphore(f"sem_{q}"))
        for i in range(NDMALANES):
            sems[(q, i)] = enter(nc.semaphore(f"sem_{q}{i}"))

    bufs = {}

    def B(name):
        if name not in bufs:
            bufs[name] = Buf(name)
        return bufs[name]

    def dma(queue, dst, src, reads=(), writes=(), partial=False, annot=""):
        return P.emit(queue, lambda e, d=dst, s=src: e.dma_start(d, s),
                      reads=reads, writes=writes, isdma=True, partial=partial,
                      annot=annot)

    ev = P.emit
    AP = bass.AP

    dbg_done = [False]
    dbg_seen = [0]

    def dbg_dump(point, tensor, buf, cols):
        if DBG["point"] != point or dbg_done[0]:
            return
        if dbg_seen[0] < DBG["skip"]:
            dbg_seen[0] += 1
            return
        dbg_done[0] = True
        dma("gpsimd", AP(hb["out"], 0, [[32768, 128], [1, cols]]), tensor[:, 0:cols],
            reads=[buf], annot=f"dbg {point}")

    for i in range(8):
        dma("gpsimd", XRES[:, i * 4 * D:(i + 1) * 4 * D],
            AP(hb["x_in"], i * 4 * 128 * D, [[D, 128], [128 * D, 4], [1, D]]),
            writes=[B("XRES")], partial=(i < 7), annot="load x")
    dma("gpsimd", IDENT[:, :], hb["ident"].ap(), writes=[B("IDENT")])
    dma("gpsimd", TRI01[:, :], hb["tri01"].ap(), writes=[B("TRI01")])
    for i, n in enumerate(("rot_csq", "rot_ssq", "rot_csk", "rot_ssk")):
        dma("gpsimd", ROT[:, i * TPS * DH:(i + 1) * TPS * DH], hb[n].ap(),
            writes=[B("ROT")])
    dma("sync", EXPB[:, 0:32], hb["expb_raw"].ap(), writes=[B("EXPB")])
    dma("sync", BFM[:, 0:8], hb["a1_bq_fm"].ap(), writes=[B("BFMq")])
    dma("sync", BFM[:, 8:16], hb["a1_bk_fm"].ap(), writes=[B("BFMk")])
    dma("sync", BFM[:, 16:24], hb["mu_r_fm"].ap(), writes=[B("BFMmr")])
    dma("sync", BFM[:, 24:32], hb["mu_k_fm"].ap(), writes=[B("BFMmk")])
    ev("vector", lambda e: e.tensor_scalar(EXPB[:, 0:32], EXPB[:, 0:32], -C1, None, ALU.add),
       reads=[B("EXPB")], writes=[B("EXPB")], small=True)
    ev("vector", lambda e: e.memset(EXPB[:, 32:33], -C1), writes=[B("EXPB")], small=True)
    ev("vector", lambda e: e.memset(EXPB[:, 33:34], -C0), writes=[B("EXPB")], small=True)
    ev("vector", lambda e: e.memset(EXPB[:, 34:35], RM), writes=[B("EXPB")], small=True)
    ev("vector", lambda e: e.memset(EXPB[:, 35:36], EPS), writes=[B("EXPB")], small=True)
    ev("vector", lambda e: e.memset(EXPB[:, 36:37], 1.0), writes=[B("EXPB")], small=True)
    ev("vector", lambda e: e.memset(ONES1[:, :], 1.0), writes=[B("ONES1")])
    ev("vector", lambda e: e.tensor_scalar(BFM[:, 32:40], BFM[:, 0:8], 0.125, None, ALU.mult),
       reads=[B("BFMq")], writes=[B("BFMq8")], small=True)
    ev("vector", lambda e: e.memset(
        AP(VH, 64, [[TPS * NH * 65, 128], [1040, TPS], [65, NH], [1, 1]]), 1.0),
        writes=[B("VH")], annot="vh ones")

    quant_work = []
    _qa_steps, _qs_steps, _qb_steps = [], [], []
    _qcols = {"f1_wgT": INNER, "f1_wuT": INNER, "f1_wdT": D}

    def pump(n):
        for _ in range(min(n, len(quant_work))):
            quant_work.pop(0)()

    if upto >= 4:
        for wname, qname, rows, cols, swc in (
                ("f1_wgT", "wgq", D, INNER, 0),
                ("f1_wuT", "wuq", D, INNER, 1),
                ("f1_wdT", "wdq", INNER, D, 2)):
            ckc = min(1024, cols)
            ncc = cols // ckc
            nrc = rows // 128
            firstbox = [True]
            for rc in range(nrc):
                for cc in range(ncc):
                    off = rc * 128 * cols + cc * ckc
                    def stepA(off=off, wname=wname, ckc=ckc, firstbox=firstbox, swc=swc):
                        ac = 217 + swc
                        dma("sync", QSTG[:, 0:ckc],
                            AP(hb[wname], off, [[_qcols[wname], 128], [1, ckc]]),
                            reads=[B("QSTG")], writes=[B("QSTG")], annot="qA")
                        ev("vector", lambda e, c=ckc: e.reduce_sum(
                            SML[:, 216:217], QSTG[:, 0:c], AX.X, apply_absolute_value=True),
                            reads=[B("QSTG")], writes=[B("WQs")], small=True)
                        if firstbox[0]:
                            firstbox[0] = False
                            ev("vector", lambda e, ac=ac: e.tensor_copy(
                                SML[:, ac:ac + 1], SML[:, 216:217]),
                               reads=[B("WQs")], writes=[B(f"WQa{swc}")], small=True)
                        else:
                            ev("vector", lambda e, ac=ac: e.tensor_add(
                                SML[:, ac:ac + 1], SML[:, ac:ac + 1], SML[:, 216:217]),
                               reads=[B("WQs"), B(f"WQa{swc}")], writes=[B(f"WQa{swc}")],
                               small=True)
                    _qa_steps.append(stepA)

            def stepS(rows=rows, cols=cols, swc=swc):
                ac = 217 + swc
                ev("vector", lambda e: e.memset(QSTG[0:1, 0:128], 1.0),
                   reads=[B("QSTG")], writes=[B("QSTG")], small=True)
                ev("tensor", lambda e, ac=ac: e.matmul(PS3[0:1, 0:1], SML[:, ac:ac + 1],
                                                       EXPB[:, 36:37], start=True, stop=True),
                   reads=[B(f"WQa{swc}"), B("EXPB")], writes=[B("PS3")], annot="wq xp")
                ev("scalar", lambda e: e.copy(SML[0:1, 220:221], PS3[0:1, 0:1]),
                   reads=[B("PS3")], writes=[B("WQt")], small=True)
                nel = float(rows * cols)
                ev("vector", lambda e, n=nel: e.tensor_scalar(
                    SML[0:1, 221:222], SML[0:1, 220:221], 1.0 / n, 1e-5, ALU.mult, ALU.max),
                    reads=[B("WQt")], writes=[B("WQm")], small=True)
                ev("vector", lambda e: e.reciprocal(SML[0:1, 222:223], SML[0:1, 221:222]),
                   reads=[B("WQm")], writes=[B("WQr")], small=True)
                ev("tensor", lambda e: e.matmul(PS3[0:128, 128:130], QSTG[0:1, 0:128],
                                                SML[0:1, 221:223], start=True, stop=True),
                   reads=[B("QSTG"), B("WQm"), B("WQr")], writes=[B("PS3")], annot="wq rep")
                ev("scalar", lambda e, c=swc: e.copy(SWS[:, c * 2:c * 2 + 2],
                                                     PS3[0:128, 128:130]),
                   reads=[B("PS3")], writes=[B(f"SWS{swc}")], annot="wq sws", small=True)
            _qs_steps.append(stepS)

            for rc in range(nrc):
                for cc in range(ncc):
                    off = rc * 128 * cols + cc * ckc
                    def stepB(off=off, wname=wname, qname=qname, ckc=ckc, swc=swc):
                        dma("sync", QSTG[:, 0:ckc],
                            AP(hb[wname], off, [[_qcols[wname], 128], [1, ckc]]),
                            reads=[B("QSTG")], writes=[B("QSTG")], annot="qB")
                        ev("vector", lambda e, c=ckc, w=swc: e.tensor_scalar(
                            QSTG[:, 0:c], QSTG[:, 0:c], SWS[:, w * 2 + 1:w * 2 + 2], RM,
                            ALU.mult, ALU.add),
                            reads=[B("QSTG")], sreads=[B(f"SWS{swc}")], writes=[B("QSTG")])
                        ev("vector", lambda e, c=ckc: e.tensor_scalar(
                            PT[:, 0:c], QSTG[:, 0:c], RM, 1.0, ALU.subtract, ALU.min),
                            reads=[B("QSTG")], writes=[B("PT0"), B("PT1")])
                        ev("vector", lambda e, c=ckc: e.tensor_scalar(
                            PT[:, 0:c], PT[:, 0:c], -1.0, None, ALU.max),
                            reads=[B("PT0"), B("PT1")], writes=[B("PT0"), B("PT1")])
                        dma("sync", AP(hb[qname], off, [[_qcols[wname], 128], [1, ckc]]),
                            PT[:, 0:ckc],
                            reads=[B("PT0"), B("PT1")], writes=[B(f"HBM{qname}")],
                            annot="qWB")
                    _qb_steps.append(stepB)

    quant_work.extend(_qa_steps + _qs_steps + _qb_steps)

    SMLW = 256

    def ln_stats(specs, slot):
        c = LNB[slot]
        n = len(specs)
        for ti, (st, sc_, sb) in enumerate(specs):
            ev("vector", lambda e, st=st, sc_=sc_, ti=ti: e.bn_stats(
                SML[:, c + 12 * ti:c + 12 * ti + 6], st[:, sc_:sc_ + 512]),
               reads=[sb], writes=[B(f"LN{slot}s{ti}a")], small=True)
            ev("vector", lambda e, st=st, sc_=sc_, ti=ti: e.bn_stats(
                SML[:, c + 12 * ti + 6:c + 12 * ti + 12], st[:, sc_ + 512:sc_ + 1024]),
               reads=[sb], writes=[B(f"LN{slot}s{ti}b")], small=True)
        for ti in range(n):
            ev("vector", lambda e, ti=ti: e.bn_aggr(
                SML[:, c + 48 + 2 * ti:c + 50 + 2 * ti], SML[:, c + 12 * ti:c + 12 * ti + 12]),
               reads=[B(f"LN{slot}s{ti}a"), B(f"LN{slot}s{ti}b")],
               writes=[B(f"LN{slot}mv{ti}")], small=True)
        mvb = [B(f"LN{slot}mv{ti}") for ti in range(n)]
        ev("scalar", lambda e, n=n: e.activation(
            SML[:, c + 56:c + 56 + n], AP(SML, c + 49, [[SMLW, 128], [2, n]]),
            AF.Ln, bias=EXPB[:, 35:36]),
           reads=mvb, sreads=[B("EXPB")], writes=[B(f"LN{slot}std")],
           small=True, annot="lnv")
        ev("scalar", lambda e, n=n: e.activation(
            SML[:, c + 60:c + 60 + n], SML[:, c + 56:c + 56 + n],
            AF.Exp, scale=-0.5),
           reads=[B(f"LN{slot}std")], writes=[B(f"LN{slot}rstd")],
           small=True, annot="rstd")
        ev("vector", lambda e, n=n: e.scalar_tensor_tensor(
            SML[:, c + 64:c + 64 + n], AP(SML, c + 48, [[SMLW, 128], [2, n]]),
            -1.0, SML[:, c + 60:c + 60 + n], ALU.mult, ALU.mult),
           reads=mvb + [B(f"LN{slot}rstd")], writes=[B(f"LN{slot}nmr")],
           small=True, annot="nmr")

    def ln_apply(ti, spec, out_ap, out_buf, slot):
        c = LNB[slot]
        st, sc_, sb = spec
        ev("scalar", lambda e, o=out_ap, st=st, sc_=sc_, ti=ti: e.activation(
            o, st[:, sc_:sc_ + 1024], AF.Identity,
            bias=SML[:, c + 64 + ti:c + 65 + ti], scale=SML[:, c + 60 + ti:c + 61 + ti]),
           reads=[sb], sreads=[B(f"LN{slot}rstd"), B(f"LN{slot}nmr")],
           writes=[out_buf], annot="ln")

    def ln_tile(src_t, src_c, src_buf, out_ap, out_buf, slot):
        spec = (src_t, src_c, src_buf)
        ln_stats([spec], slot)
        ln_apply(0, spec, out_ap, out_buf, slot)

    def transposes_to(dst_t, dst_buf, dstcol, src_t, src_c0, src_buf, nch=8):
        for ci in range(nch):
            ev("tensor", lambda e, ci=ci: e.transpose(
                PST[:, ci * 128:(ci + 1) * 128],
                src_t[:, src_c0 + ci * 128:src_c0 + (ci + 1) * 128], IDENT[:, :]),
                reads=[src_buf, B("IDENT")], writes=[B("PST")], annot="tr")
        d0 = dstcol(0)
        stride = (dstcol(1) - d0) if nch > 1 else 128
        ev("scalar", lambda e: e.copy(
            AP(dst_t, d0, [[dst_t.shape[1], 128], [stride, nch], [1, 128]]),
            AP(PST, 0, [[1024, 128], [128, nch], [1, 128]])),
            reads=[B("PST")], writes=[dst_buf], annot="trcp")

    def xnt_slot(s, ch):
        return (s * NQB + ch) % 2

    def gen_xnt_stats(s, ch):
        slot = xnt_slot(s, ch)
        specs = [(XRES, (s * TPS + ch * 4 + ti) * D, B("XRES")) for ti in range(4)]
        ln_stats(specs, slot)
        for ti in range(4):
            ln_apply(ti, specs[ti], SC1[:, ti * 1024:(ti + 1) * 1024],
                     B(f"SC1{ti}"), slot)
        return specs

    def gen_xnt_apply(s, ch, specs):
        for ti in range(4):
            for ci in range(8):
                ev("tensor", lambda e, ci=ci, ti=ti: e.transpose(
                    PST[:, ci * 128:(ci + 1) * 128],
                    SC1[:, ti * 1024 + ci * 128:ti * 1024 + (ci + 1) * 128],
                    IDENT[:, :]),
                    reads=[B(f"SC1{ti}"), B("IDENT")], writes=[B("PST")], annot="tr")
            ev("scalar", lambda e, ti=ti: e.copy(
                AP(XNT, ti * 128, [[KCH * QB, 128], [QB, KCH], [1, 128]]),
                AP(PST, 0, [[1024, 128], [128, KCH], [1, 128]])),
                reads=[B("PST")], writes=[B("XNT")], annot="trcp")
        dbg_dump("xnt", XNT, B("XNT"), 4096)

    def gen_xnt_chunk(s, ch):
        gen_xnt_apply(s, ch, gen_xnt_stats(s, ch))

    def load_weight(wname):
        for i in range(2):
            dma("gpsimd", WA[:, i * 4 * D:(i + 1) * 4 * D],
                AP(hb[wname], i * 4 * 128 * D, [[D, 128], [128 * D, 4], [1, D]]),
                reads=[B("WA")], writes=[B("WA")], partial=(i < 1),
                annot=f"w {wname}")

    def fm_proj(srcT, src_buf, m, nq, dst_ap, dst_buf, bias_ap, bias_buf, scale):
        pq = B(f"PS1q{nq}")
        for k in range(KCH):
            ev("tensor", lambda e, m=m, k=k, nq=nq, st=(k == 0): e.matmul(
                AP(PS1, nq * 512, [[2048, 128], [1, QB]]),
                WA[:, k * D + m * 128:k * D + (m + 1) * 128],
                srcT[:, k * QB:(k + 1) * QB], start=st, stop=(k == KCH - 1)),
                reads=[B("WA"), src_buf], writes=[pq], annot="fmp")
        ev("scalar", lambda e, nq=nq, d=dst_ap, b=bias_ap, sc=scale: e.activation(
            d, AP(PS1, nq * 512, [[2048, 128], [1, QB]]), AF.Identity, bias=b, scale=sc),
            reads=[pq], sreads=[bias_buf], writes=[dst_buf], annot="fmev")

    def tok_proj(stat_t, stat_buf, stat_c, half_ps, psbufs):
        for k in range(KCH):
            for hf in range(2):
                rhs = WA[:, k * D + hf * 512:k * D + (hf + 1) * 512]
                ev("tensor", lambda e, k=k, hf=hf, r=rhs, st=(k == 0): e.matmul(
                    AP(PS1, half_ps * 1024 + hf * 512, [[2048, 128], [1, 512]]),
                    stat_t[:, stat_c + k * QB:stat_c + k * QB + 128],
                    r, start=st, stop=(k == KCH - 1)),
                    reads=[B("WA"), stat_buf], writes=[psbufs[hf]], annot="tkp")

    def attn_block(layer):
        l0 = layer == 0
        pre = "a0_" if l0 else "a1_"
        if not l0:
            pump(len(quant_work))
        for s in range(SEQ_PC):
            load_weight(pre + "wkT")
            if l0:
                dma("gpsimd", BREP[:, :], hb[pre + "bk_rep"].ap(),
                    reads=[B("BREP")], writes=[B("BREP")])
            kspecs = gen_xnt_stats(s, 0)
            for ch in range(NQB):
                pump(2)
                gen_xnt_apply(s, ch, kspecs)
                if ch + 1 < NQB:
                    kspecs = gen_xnt_stats(s, ch + 1)
                if l0:
                    for ti in range(4):
                        tt = ch * 4 + ti
                        _rot_path(s, tt, ti, "k", KT, B("KT"),
                                  lambda k, t=tt: k * S + t * 128)
                else:
                    for m in range(KCH):
                        fm_proj(XNT, B("XNT"), m, m % 2,
                                KT[:, m * S + ch * QB:m * S + (ch + 1) * QB], B("KT"),
                                BFM[:, 8 + m:9 + m], B("BFMk"), 1.0)
            dbg_dump("kt", KT, B("KT"), 16384)
            load_weight(pre + "wvT")
            dma("gpsimd", BREP[:, :], hb[pre + "bv_rep"].ap(),
                reads=[B("BREP")], writes=[B("BREP")])
            vspecs = gen_xnt_stats(s, 0)
            for ch in range(NQB):
                pump(2)
                gen_xnt_apply(s, ch, vspecs)
                if ch + 1 < NQB:
                    vspecs = gen_xnt_stats(s, ch + 1)
                for ti in range(4):
                    tt = ch * 4 + ti
                    hp = tt % 2
                    pbs = [B(f"PS1q{2 * hp}"), B(f"PS1q{2 * hp + 1}")]
                    tok_proj(XNT, B("XNT"), ti * 128, hp, pbs)
                    ev("vector", lambda e, tt=tt, hp=hp: e.tensor_tensor(
                        AP(VH, tt * NH * 65, [[TPS * NH * 65, 128], [65, NH], [1, 64]]),
                        AP(PS1, hp * 1024, [[2048, 128], [64, NH], [1, 64]]),
                        AP(BREP, 0, [[D, 128], [64, NH], [1, 64]]), ALU.add),
                        reads=pbs + [B("BREP")], writes=[B("VH")], annot="vev")
            dbg_dump("vh", VH, B("VH"), 16640)
            load_weight(pre + "wqT")
            if l0:
                dma("gpsimd", BREP[:, :], hb[pre + "bq_rep"].ap(),
                    reads=[B("BREP")], writes=[B("BREP")])
            dma("gpsimd", BREP2[:, :], hb[pre + "bo_rep"].ap(),
                reads=[B("BREP2")], writes=[B("BREP2")])
            qspecs = gen_xnt_stats(s, 0)
            for qb in range(NQB):
                pump(2)
                gen_xnt_apply(s, qb, qspecs)
                if qb + 1 < NQB:
                    qspecs = gen_xnt_stats(s, qb + 1)
                if l0:
                    for ti in range(4):
                        _rot_path(s, qb * 4 + ti, ti, "q", QT, B("QT"),
                                  lambda k, t=ti: k * QB + t * 128)
                else:
                    for m in range(KCH):
                        fm_proj(XNT, B("XNT"), m, m % 2,
                                QT[:, m * QB:(m + 1) * QB], B("QT"),
                                BFM[:, 32 + m:33 + m], B("BFMq8"), 0.125)
                dbg_dump("qt", QT, B("QT"), 4096)
                load_weight(pre + "woT")
                ev("vector", lambda e: e.memset(QSTG[0:1, 0:64], 1.0),
                   reads=[B("QSTG")], writes=[B("QSTG")], small=True)
                for h0 in range(0, NH, 2):
                    _attn_pair(l0, s, qb, h0,
                               tailfn=(lambda h0=h0: _attn_pair_tail(h0 - 2))
                               if h0 > 0 else None)
                _attn_pair_tail(NH - 2)
                _wo_ln_res(pre, s, qb, last_qb=(qb == NQB - 1))
                dbg_dump("wo", SC2, B("SC2"), 1024)

    def _rot_path(s, tt, ti, which, dstT, dst_buf, dstcol):
        hp = tt % 2
        pbs = [B(f"PS1q{2 * hp}"), B(f"PS1q{2 * hp + 1}")]
        tok_proj(XNT, B("XNT"), ti * 128, hp, pbs)
        ev("vector", lambda e, hp=hp: e.tensor_tensor(
            SC3[:, :], PS1[:, hp * 1024:(hp + 1) * 1024], BREP[:, :], ALU.add),
            reads=pbs + [B("BREP")], writes=[B("SC3")], annot="rotev")
        dbg_dump("rotev", SC3, B("SC3"), 1024)
        ev("vector", lambda e: e.tensor_copy(
            AP(QSTG, 0, [[1024, 128], [2, 512]]), AP(SC3, 1, [[D, 128], [2, 512]])),
            reads=[B("SC3")], writes=[B("QSTG")])
        ev("vector", lambda e: e.tensor_copy(
            AP(QSTG, 1, [[1024, 128], [2, 512]]), AP(SC3, 0, [[D, 128], [2, 512]])),
            reads=[B("SC3")], writes=[B("QSTG")])
        ro = 0 if which == "q" else 2
        cs_off = ro * TPS * DH + tt * DH
        ss_off = (ro + 1) * TPS * DH + tt * DH
        ev("vector", lambda e, o=ss_off: e.tensor_tensor(
            QSTG[:, 0:1024], QSTG[:, 0:1024], AP(ROT, o, [[4 * TPS * DH, 128], [0, NH], [1, DH]]),
            ALU.mult), reads=[B("QSTG"), B("ROT")], writes=[B("QSTG")])
        ev("vector", lambda e, o=cs_off: e.tensor_tensor(
            SC3[:, :], SC3[:, :], AP(ROT, o, [[4 * TPS * DH, 128], [0, NH], [1, DH]]),
            ALU.mult), reads=[B("SC3"), B("ROT")], writes=[B("SC3")])
        ev("vector", lambda e: e.tensor_add(
            PT[:, 0:1024], SC3[:, :], QSTG[:, 0:1024]),
           reads=[B("SC3"), B("QSTG")], writes=[B("PT0"), B("PT1")])
        transposes_to(dstT, dst_buf, dstcol, PT, 0, B("PT0"))

    def _attn_pair(l0, s, qb, h0, tailfn=None):
        qb0 = qb * QB
        nkt = (4 * qb + 4) if l0 else TPS

        def hq(h):
            return 0 if h % 2 == 0 else 2

        def pvsel(h):
            if h % 2 == 0:
                return (lambda c0: PS3[0:65, c0:QB] if c0 else PS3[0:65, 0:QB],
                        B("PS3"))
            return (lambda c0: AP(PS2, 512 + c0, [[1024, 65], [1, QB - c0]]),
                    B("PS2h1"))

        if not l0:
            for h in (h0, h0 + 1):
                bslot = h % 2
                dma("gpsimd", ROT[:, bslot * 384:(bslot + 1) * 384],
                    AP(hb["band"], h * 384, [[16 * 384, 128], [1, 384]]),
                    reads=[B(f"BAND{bslot}"), B("ROT")],
                    writes=[B(f"BAND{bslot}"), B("ROT")], annot="band")

        def emit_sT(h, kt):
            kt0 = kt * 128
            c0 = max(0, kt0 - qb0) if l0 else 0
            hp = 64 * (h % 2)
            hc = h // 2
            nq = hq(h) + (kt % 2)
            ev("tensor", lambda e, kt=kt, c0=c0, o=nq * 512, hp=hp, hc=hc: e.matmul(
                AP(PS1, o + c0, [[2048, 128], [1, QB - c0]]),
                AP(KT, hp * KCH * S + hc * S + kt * 128, [[KCH * S, 64], [1, 128]]),
                AP(QT, hp * KCH * QB + hc * QB + c0, [[KCH * QB, 64], [1, QB - c0]]),
                start=True, stop=True),
                reads=[B("KT"), B("QT")], writes=[B(f"PS1q{nq}")], annot="sT")

        def emit_mid(h, kt):
            kt0 = kt * 128
            c0 = max(0, kt0 - qb0) if l0 else 0
            nq = hq(h) + (kt % 2)
            pS = B(f"PS1q{nq}")
            pso = nq * 512
            pt = hq(h) + (kt % 2)
            if l0:
                ev("scalar", lambda e, c0=c0, o=pso, pt=pt: e.activation(
                    PT[:, pt * QB + c0:(pt + 1) * QB],
                    AP(PS1, o + c0, [[2048, 128], [1, QB - c0]]),
                    AF.Exp, bias=EXPB[:, 33:34], scale=1.0 / 32.0),
                    reads=[pS], sreads=[B("EXPB")], writes=[B(f"PT{pt}")], annot="exp0")
                if kt0 >= qb0:
                    dc0 = kt0 - qb0
                    ev("vector", lambda e, dc0=dc0, pt=pt: e.tensor_tensor(
                        PT[:, pt * QB + dc0:pt * QB + dc0 + 128],
                        PT[:, pt * QB + dc0:pt * QB + dc0 + 128], TRI01[:, :], ALU.mult),
                        reads=[B(f"PT{pt}"), B("TRI01")], writes=[B(f"PT{pt}")],
                        annot="mask")
            else:
                lo = min(QB, max(0, kt0 - 128 - qb0))
                hi = min(QB, max(0, kt0 + 256 - qb0))
                bslot = h % 2
                if lo < hi:
                    jj0 = qb0 + lo - (kt0 - 128)
                    ev("vector", lambda e, lo=lo, hi=hi, j=jj0, o=pso, bs=bslot: e.tensor_tensor(
                        AP(PS1, o + lo, [[2048, 128], [1, hi - lo]]),
                        AP(PS1, o + lo, [[2048, 128], [1, hi - lo]]),
                        ROT[:, bs * 384 + j:bs * 384 + j + (hi - lo)], ALU.add),
                        reads=[pS, B(f"BAND{bslot}")], writes=[pS], annot="badd")
                if lo > 0:
                    ev("scalar", lambda e, lo=lo, o=pso, pt=pt, h=h: e.activation(
                        PT[:, pt * QB:pt * QB + lo], AP(PS1, o, [[2048, 128], [1, lo]]),
                        AF.Exp, bias=EXPB[:, 2 * h:2 * h + 1]),
                        reads=[pS], sreads=[B("EXPB")], writes=[B(f"PT{pt}")], annot="expL")
                if lo < hi:
                    ev("scalar", lambda e, lo=lo, hi=hi, o=pso, pt=pt: e.activation(
                        PT[:, pt * QB + lo:pt * QB + hi],
                        AP(PS1, o + lo, [[2048, 128], [1, hi - lo]]),
                        AF.Exp, bias=EXPB[:, 32:33]),
                        reads=[pS], sreads=[B("EXPB")], writes=[B(f"PT{pt}")], annot="expM")
                if hi < QB:
                    ev("scalar", lambda e, hi=hi, o=pso, pt=pt, h=h: e.activation(
                        PT[:, pt * QB + hi:(pt + 1) * QB],
                        AP(PS1, o + hi, [[2048, 128], [1, QB - hi]]),
                        AF.Exp, bias=EXPB[:, 2 * h + 1:2 * h + 2]),
                        reads=[pS], sreads=[B("EXPB")], writes=[B(f"PT{pt}")], annot="expR")

        def emit_pv(h, kt):
            kt0 = kt * 128
            c0 = max(0, kt0 - qb0) if l0 else 0
            pt = hq(h) + (kt % 2)
            pvdst, pvbuf = pvsel(h)
            ev("tensor", lambda e, kt=kt, c0=c0, pt=pt, h=h, st=(kt == 0),
               sp=(kt == nkt - 1): e.matmul(
                pvdst(c0),
                AP(VH, kt * NH * 65 + h * 65, [[TPS * NH * 65, 128], [1, 65]]),
                PT[:, pt * QB + c0:(pt + 1) * QB], start=st, stop=sp),
                reads=[B(f"PT{pt}"), B("VH")], writes=[pvbuf], annot="pvT")

        for h in (h0, h0 + 1):
            emit_sT(h, 0)
            emit_mid(h, 0)
        if tailfn is not None:
            tailfn()
        for kt in range(nkt):
            for h in (h0, h0 + 1):
                if kt + 1 < nkt:
                    emit_sT(h, kt + 1)
                    emit_mid(h, kt + 1)
            emit_pv(h0, kt)
            emit_pv(h0 + 1, kt)

    def _attn_pair_tail(h0):
        for h in (h0, h0 + 1):
            hp = 64 * (h % 2)
            hc = h // 2
            if h % 2 == 0:
                pvbuf = B("PS3")
                pvrow = PS3[64:65, 0:QB]
                pvval = PS3[0:64, 0:QB]
            else:
                pvbuf = B("PS2h1")
                pvrow = AP(PS2, 64 * 1024 + 512, [[1024, 1], [1, QB]])
                pvval = AP(PS2, 512, [[1024, 64], [1, QB]])
            ev("scalar", lambda e, r=pvrow: e.copy(QSTG[0:1, 64:64 + QB], r),
               reads=[pvbuf], writes=[B("QSTG")], annot="sums", small=True)
            ev("tensor", lambda e: e.matmul(
                PS2[0:64, 0:QB], QSTG[0:1, 0:64], QSTG[0:1, 64:64 + QB],
                start=True, stop=True),
                reads=[B("QSTG")], writes=[B("PS2h0")], annot="srep")
            ev("scalar", lambda e: e.activation(SC3[0:64, 0:QB], PS2[0:64, 0:QB], AF.Ln),
               reads=[B("PS2h0")], writes=[B("SC3")], annot="lnden")
            ev("scalar", lambda e: e.activation(SC3[0:64, 0:QB], SC3[0:64, 0:QB],
                                                AF.Exp, scale=-1.0),
               reads=[B("SC3")], writes=[B("SC3")], annot="recden")
            ev("vector", lambda e, hp=hp, hc=hc, v=pvval: e.tensor_tensor(
                AP(INT, hp * KCH * QB + hc * QB, [[KCH * QB, 64], [1, QB]]),
                v, SC3[0:64, 0:QB], ALU.mult),
                reads=[pvbuf, B("SC3")], writes=[B("INT")], annot="onorm")

    def _wo_ln_res(pre, s, qb, last_qb):
        for pair in range(2):
            pbs = [B(f"PS1q{i}") for i in range(4)]
            for k in range(KCH):
                for t2 in range(2):
                    ti = pair * 2 + t2
                    for hf in range(2):
                        ev("tensor", lambda e, k=k, ti=ti, t2=t2, hf=hf,
                           st=(k == 0): e.matmul(
                            AP(PS1, t2 * 1024 + hf * 512, [[2048, 128], [1, 512]]),
                            INT[:, k * QB + ti * 128:k * QB + (ti + 1) * 128],
                            WA[:, k * D + hf * 512:k * D + (hf + 1) * 512],
                            start=st, stop=(k == KCH - 1)),
                            reads=[B("WA"), B("INT")],
                            writes=[pbs[t2 * 2 + hf]], annot="wo")
            for t2 in range(2):
                ti = pair * 2 + t2
                g = s * TPS + qb * 4 + ti
                ev("vector", lambda e, t2=t2: e.tensor_tensor(
                    SC2[:, :], PS1[:, t2 * 1024:(t2 + 1) * 1024], BREP2[:, :], ALU.add),
                    reads=[pbs[t2 * 2], pbs[t2 * 2 + 1], B("BREP2")], writes=[B("SC2")],
                    annot="woev")
                if pair == 1 and t2 == 0 and not last_qb:
                    load_weight(pre + "wqT")
                _ln_res_store(g, last=False)

    def _ln_res_store(g, last):
        ln_tile(SC2, 0, B("SC2"), SC3[:, :], B("SC3"), slot=2)
        if not last:
            ev("vector", lambda e, g=g: e.tensor_add(
                XRES[:, g * D:(g + 1) * D], XRES[:, g * D:(g + 1) * D], SC3[:, :]),
                reads=[B("SC3"), B("XRES")], writes=[B("XRES")], annot="res+")
        else:
            ev("vector", lambda e, g=g: e.tensor_add(
                QSTG[:, 0:1024], XRES[:, g * D:(g + 1) * D], SC3[:, :]),
                reads=[B("SC3"), B("XRES")], writes=[B("QSTG")], annot="resO")
            dma("sync", AP(hb["out"], g * 128 * D, [[D, 128], [1, D]]), QSTG[:, 0:1024],
                reads=[B("QSTG")], annot="store")

    def ffn0_block():
        load_weight("f0_wkT")
        for s in range(SEQ_PC):
            fspecs = gen_xnt_stats(s, 0)
            for ch in range(NQB):
                pump(6)
                gen_xnt_apply(s, ch, fspecs)
                if ch + 1 < NQB:
                    fspecs = gen_xnt_stats(s, ch + 1)
                for k in range(KCH):
                    ev("vector", lambda e, k=k: e.tensor_sub(
                        SC2[:, 1:512], XNT[:, k * QB + 1:(k + 1) * QB],
                        XNT[:, k * QB:(k + 1) * QB - 1]),
                        reads=[B("XNT")], writes=[B("SC2")], annot="fdx")
                    for dst, dbuf, mub, mc in ((QT, B("QT"), B("BFMmk"), 24),
                                               (INT, B("INT"), B("BFMmr"), 16)):
                        ev("vector", lambda e, k=k, dst=dst, mc=mc: e.scalar_tensor_tensor(
                            dst[:, k * QB + 1:(k + 1) * QB], SC2[:, 1:512],
                            BFM[:, mc + k:mc + k + 1],
                            XNT[:, k * QB:(k + 1) * QB - 1], ALU.mult, ALU.add),
                            reads=[B("SC2"), B("XNT")], sreads=[mub], writes=[dbuf],
                            annot="fmix")
                        if ch == 0:
                            ev("vector", lambda e, k=k, dst=dst, mc=mc: e.tensor_scalar(
                                dst[:, k * QB:k * QB + 1], XNT[:, k * QB:k * QB + 1],
                                BFM[:, mc + k:mc + k + 1], None, ALU.mult),
                                reads=[B("XNT")], sreads=[mub], writes=[dbuf], annot="fc0")
                        else:
                            ev("vector", lambda e, k=k: e.tensor_sub(
                                SC3[:, 0:1], XNT[:, k * QB:k * QB + 1],
                                BFM[:, 40 + k:41 + k]),
                                reads=[B("XNT"), B("PREV")], writes=[B("SC3")], annot="fd0")
                            ev("vector", lambda e, k=k, dst=dst, mc=mc: e.scalar_tensor_tensor(
                                dst[:, k * QB:k * QB + 1], SC3[:, 0:1],
                                BFM[:, mc + k:mc + k + 1],
                                BFM[:, 40 + k:41 + k], ALU.mult, ALU.add),
                                reads=[B("SC3"), B("PREV")], sreads=[mub], writes=[dbuf],
                                annot="fm0")
                ev("vector", lambda e: e.tensor_copy(
                    AP(BFM, 40, [[48, 128], [1, KCH]]),
                    AP(XNT, QB - 1, [[KCH * QB, 128], [QB, KCH], [1, 1]])),
                    reads=[B("XNT")], writes=[B("PREV")], annot="fprev", small=True)
                for m in range(KCH):
                    nq = m % 2
                    pq = B(f"PS1q{nq}")
                    for k in range(KCH):
                        ev("tensor", lambda e, m=m, k=k, nq=nq, st=(k == 0): e.matmul(
                            AP(PS1, nq * 512, [[2048, 128], [1, QB]]),
                            WA[:, k * D + m * 128:k * D + (m + 1) * 128],
                            QT[:, k * QB:(k + 1) * QB], start=st, stop=(k == KCH - 1)),
                            reads=[B("WA"), B("QT")], writes=[pq], annot="f0k")
                    ev("vector", lambda e, nq=nq: e.tensor_scalar(
                        SC2[:, 0:512], AP(PS1, nq * 512, [[2048, 128], [1, QB]]),
                        0.0, None, ALU.max),
                        reads=[pq], writes=[B("SC2")], annot="f0rl")
                    ev("vector", lambda e, m=m: e.tensor_mul(
                        KT[:, m * QB:(m + 1) * QB], SC2[:, 0:512], SC2[:, 0:512]),
                        reads=[B("SC2")], writes=[B("KT")], annot="f0sq")
                for pair in range(2):
                    for wname, srcT, sbuf, ann in (("f0_wrT", INT, B("INT"), "f0r"),
                                                   ("f0_wvT", KT, B("KT"), "f0v")):
                        isv = wname == "f0_wvT"
                        load_weight(wname)
                        for k in range(KCH):
                            for t2 in range(2):
                                ti = pair * 2 + t2
                                for hf in range(2):
                                    ev("tensor", lambda e, k=k, ti=ti, t2=t2, hf=hf,
                                       srcT=srcT, st=(k == 0): e.matmul(
                                        AP(PS1, t2 * 1024 + hf * 512, [[2048, 128], [1, 512]]),
                                        srcT[:, k * QB + ti * 128:k * QB + (ti + 1) * 128],
                                        WA[:, k * D + hf * 512:k * D + (hf + 1) * 512],
                                        start=st, stop=(k == KCH - 1)),
                                        reads=[B("WA"), sbuf],
                                        writes=[B(f"PS1q{t2 * 2 + hf}")], annot=ann)
                        for t2 in range(2):
                            pq = [B(f"PS1q{t2 * 2}"), B(f"PS1q{t2 * 2 + 1}")]
                            if t2 == 0:
                                rdst, rbufs = PT[:, 0:1024], [B("PT0"), B("PT1")]
                            else:
                                rdst, rbufs = KT[:, 4096:5120], [B("KTr")]
                            if not isv:
                                ev("scalar", lambda e, t2=t2, rd=rdst: e.activation(
                                    rd, PS1[:, t2 * 1024:(t2 + 1) * 1024], AF.Sigmoid),
                                    reads=pq, writes=rbufs, annot="f0sg")
                            else:
                                ti = pair * 2 + t2
                                g = s * TPS + ch * 4 + ti
                                ev("vector", lambda e, t2=t2, rd=rdst: e.tensor_mul(
                                    SC2[:, :], PS1[:, t2 * 1024:(t2 + 1) * 1024], rd),
                                    reads=pq + rbufs,
                                    writes=[B("SC2")], annot="f0m")
                                _ln_res_store(g, last=False)
                if not (s == SEQ_PC - 1 and ch == NQB - 1):
                    load_weight("f0_wkT")

    def ffn1_block(last):
        for sb_i in range(SEQ_PC * S // QB):
            s = sb_i // NQB
            ch = sb_i % NQB
            for ti in range(4):
                g = s * TPS + ch * 4 + ti
                ln_tile(XRES, g * D, B("XRES"), SC2[:, :], B("SC2"), slot=0)
                ev("vector", lambda e: e.reduce_max(SML[:, 224:225], SC2[:, :], AX.X,
                                                    apply_absolute_value=True),
                   reads=[B("SC2")], writes=[B("QA")], small=True)
                ev("vector", lambda e: e.tensor_scalar(SML[:, 225:226], SML[:, 224:225],
                                                       1e-5, None, ALU.max),
                   reads=[B("QA")], writes=[B("QAc")], small=True)
                ev("vector", lambda e: e.reciprocal(SML[:, 226:227], SML[:, 225:226]),
                   reads=[B("QAc")], writes=[B("QAr")], small=True)
                ev("vector", lambda e: e.tensor_scalar(SML[:, 227:228], SML[:, 226:227],
                                                       127.0, None, ALU.mult),
                   reads=[B("QAr")], writes=[B("QAr127")], small=True)
                ev("vector", lambda e: e.tensor_scalar(SML[:, 228:229], SML[:, 225:226],
                                                       1.0 / 127.0, None, ALU.mult),
                   reads=[B("QAc")], writes=[B("QAd")], small=True)
                for j in range(2):
                    ev("vector", lambda e, j=j, ti=ti: e.tensor_tensor(
                        SML[:, 230 + ti * 3 + j:231 + ti * 3 + j], SML[:, 228:229],
                        SWS[:, j * 2:j * 2 + 1], ALU.mult),
                        reads=[B("QAd"), B(f"SWS{j}")], writes=[B(f"TC{ti}_{j}")],
                        annot="comb", small=True)
                ev("scalar", lambda e: e.activation(SC3[:, :], SC2[:, :], AF.Identity,
                                                    bias=EXPB[:, 34:35],
                                                    scale=SML[:, 227:228]),
                   reads=[B("SC2")], sreads=[B("QAr127"), B("EXPB")], writes=[B("SC3")],
                   annot="xq1")
                dbg_dump("f1sc2", SC2, B("SC2"), 1024)
                dbg_dump("f1sc3", SC3, B("SC3"), 1024)
                ev("vector", lambda e: e.tensor_scalar(
                    PT[:, 0:1024], SC3[:, :], RM,
                    127.0, ALU.subtract, ALU.min),
                   reads=[B("SC3")], writes=[B("PT0"), B("PT1")])
                ev("vector", lambda e: e.tensor_scalar(
                    PT[:, 0:1024], PT[:, 0:1024], -128.0, None, ALU.max),
                   reads=[B("PT0"), B("PT1")], writes=[B("PT0"), B("PT1")])
                transposes_to(QT, B("QT"), lambda k, t=ti: k * QB + t * 128, PT,
                              0, B("PT0"))
            dbg_dump("xq", QT, B("QT"), 4096)
            for ic in range(INNER // 512):
                for qn, wsl, wbuf in (("wgq", XNT, B("XNT")), ("wuq", INT, B("INT"))):
                    for i in range(2):
                        dma("sync", wsl[:, i * 4 * 512:(i + 1) * 4 * 512],
                            AP(hb[qn], ic * 512 + i * 4 * 128 * INNER,
                               [[INNER, 128], [128 * INNER, 4], [1, 512]]),
                            reads=[wbuf, B(f"HBM{qn}")], writes=[wbuf],
                            partial=(i < 1), annot="f1w")
                for ti in range(4):
                    nqg = ti % 2
                    nqu = 2 + ti % 2
                    for k in range(KCH):
                        ev("tensor", lambda e, k=k, ti=ti, nq=nqg, st=(k == 0): e.matmul(
                            AP(PS1, nq * 512, [[2048, 128], [1, 512]]),
                            QT[:, k * QB + ti * 128:k * QB + (ti + 1) * 128],
                            XNT[:, k * 512:(k + 1) * 512], start=st, stop=(k == KCH - 1)),
                            reads=[B("XNT"), B("QT")], writes=[B(f"PS1q{nqg}")],
                            annot="f1g")
                        ev("tensor", lambda e, k=k, ti=ti, nq=nqu, st=(k == 0): e.matmul(
                            AP(PS1, nq * 512, [[2048, 128], [1, 512]]),
                            QT[:, k * QB + ti * 128:k * QB + (ti + 1) * 128],
                            INT[:, k * 512:(k + 1) * 512], start=st, stop=(k == KCH - 1)),
                            reads=[B("INT"), B("QT")], writes=[B(f"PS1q{nqu}")],
                            annot="f1u")
                    ev("scalar", lambda e, ti=ti, nq=nqg: e.activation(
                        SC3[:, 0:512], AP(PS1, nq * 512, [[2048, 128], [1, 512]]),
                        AF.Silu, scale=SML[:, 230 + ti * 3:231 + ti * 3]),
                        reads=[B(f"PS1q{nqg}")], sreads=[B(f"TC{ti}_0")],
                        writes=[B("SC3")], annot="f1si")
                    ev("scalar", lambda e, ti=ti, nq=nqu: e.activation(
                        QSTG[:, 0:512], AP(PS1, nq * 512, [[2048, 128], [1, 512]]),
                        AF.Identity, scale=SML[:, 231 + ti * 3:232 + ti * 3]),
                        reads=[B(f"PS1q{nqu}")], sreads=[B(f"TC{ti}_1")],
                        writes=[B("QSTG")], annot="f1up")
                    ev("vector", lambda e, ti=ti, ic=ic: e.tensor_mul(
                        KT[:, ti * INNER + ic * 512:ti * INNER + (ic + 1) * 512],
                        SC3[:, 0:512], QSTG[:, 0:512]),
                        reads=[B("SC3"), B("QSTG")], writes=[B("KT")], annot="f1h")
                    hm = B(f"HM{ti}")
                    if ic == 0:
                        ev("vector", lambda e, ti=ti, ic=ic: e.reduce_max(
                            SML[:, 242 + ti:243 + ti],
                            KT[:, ti * INNER + ic * 512:ti * INNER + (ic + 1) * 512],
                            AX.X, apply_absolute_value=True),
                            reads=[B("KT")], writes=[hm], annot="f1hm")
                    else:
                        ev("vector", lambda e, ti=ti, ic=ic: e.reduce_max(
                            SML[:, 246:247],
                            KT[:, ti * INNER + ic * 512:ti * INNER + (ic + 1) * 512],
                            AX.X, apply_absolute_value=True),
                            reads=[B("KT")], writes=[B("HMt")], annot="f1hm", small=True)
                        ev("vector", lambda e, ti=ti: e.tensor_tensor(
                            SML[:, 242 + ti:243 + ti], SML[:, 242 + ti:243 + ti],
                            SML[:, 246:247], ALU.max),
                            reads=[B("HMt"), hm], writes=[hm])
            dbg_dump("f1h", KT, B("KT"), 16384)
            for ti in range(4):
                ev("vector", lambda e, ti=ti: e.tensor_scalar(
                    SML[:, 247:248], SML[:, 242 + ti:243 + ti], 1e-5, None, ALU.max),
                    reads=[B(f"HM{ti}")], writes=[B("HXc")], small=True)
                ev("vector", lambda e: e.reciprocal(SML[:, 248:249], SML[:, 247:248]),
                   reads=[B("HXc")], writes=[B("HXr")], small=True)
                ev("vector", lambda e: e.tensor_scalar(SML[:, 249:250], SML[:, 248:249],
                                                       127.0, None, ALU.mult),
                   reads=[B("HXr")], writes=[B("HXr127")], small=True)
                ev("vector", lambda e: e.tensor_scalar(SML[:, 250:251], SML[:, 247:248],
                                                       1.0 / 127.0, None, ALU.mult),
                   reads=[B("HXc")], writes=[B("HXd")], small=True)
                ev("vector", lambda e, ti=ti: e.tensor_tensor(
                    SML[:, 232 + ti * 3:233 + ti * 3], SML[:, 250:251], SWS[:, 4:5],
                    ALU.mult),
                    reads=[B("HXd"), B("SWS2")], writes=[B(f"TC{ti}_2")], annot="combd", small=True)
                for hh in range(4):
                    ev("scalar", lambda e, ti=ti, hh=hh: e.activation(
                        QSTG[:, :],
                        KT[:, ti * INNER + hh * 1024:ti * INNER + (hh + 1) * 1024],
                        AF.Identity, bias=EXPB[:, 34:35], scale=SML[:, 249:250]),
                        reads=[B("KT")], sreads=[B("HXr127"), B("EXPB")],
                        writes=[B("QSTG")], annot="hq1")
                    ev("vector", lambda e, ti=ti, hh=hh: e.tensor_scalar(
                        KT[:, ti * INNER + hh * 1024:ti * INNER + (hh + 1) * 1024],
                        QSTG[:, :], RM, 127.0, ALU.subtract, ALU.min),
                        reads=[B("QSTG")], writes=[B("KT")], annot="hq2")
                ev("vector", lambda e, ti=ti: e.tensor_scalar(
                    KT[:, ti * INNER:(ti + 1) * INNER],
                    KT[:, ti * INNER:(ti + 1) * INNER], -128.0, None, ALU.max),
                    reads=[B("KT")], writes=[B("KT")], annot="hq3")
                for batch in range(4):
                    for ci in range(8):
                        kc = batch * 8 + ci
                        ev("tensor", lambda e, ti=ti, kc=kc, ci=ci: e.transpose(
                            PST[:, ci * 128:(ci + 1) * 128],
                            KT[:, ti * INNER + kc * 128:ti * INNER + (kc + 1) * 128],
                            IDENT[:, :]),
                            reads=[B("KT"), B("IDENT")], writes=[B("PST")], annot="hqt")
                    ev("scalar", lambda e, ti=ti, batch=batch: e.copy(
                        AP(VH, batch * 8 * 512 + ti * 128,
                           [[TPS * NH * 65, 128], [512, 8], [1, 128]]),
                        AP(PST, 0, [[1024, 128], [128, 8], [1, 128]])),
                        reads=[B("PST")], writes=[B("VH")], annot="hqc")
            dbufs = [(XNT, B("XNT")), (INT, B("INT")), (QT, B("QT"))]
            for pair in range(2):
                for piece in range(8):
                    dt_, dbuf = dbufs[piece % 3]
                    dma("sync", dt_[:, :],
                        AP(hb["wdq"], piece * 512 * D, [[D, 128], [128 * D, 4], [1, D]]),
                        reads=[dbuf, B("HBMwdq")], writes=[dbuf], annot="f1dw")
                    for sub in range(4):
                        kc = piece * 4 + sub
                        for t2 in range(2):
                            ti = pair * 2 + t2
                            for hf in range(2):
                                ev("tensor", lambda e, kc=kc, sub=sub, ti=ti, t2=t2,
                                   hf=hf, dt_=dt_, st=(kc == 0): e.matmul(
                                    AP(PS1, t2 * 1024 + hf * 512, [[2048, 128], [1, 512]]),
                                    AP(VH, kc * 512 + ti * 128, [[TPS * NH * 65, 128], [1, 128]]),
                                    dt_[:, sub * D + hf * 512:sub * D + (hf + 1) * 512],
                                    start=st, stop=(kc == INNER // 128 - 1)),
                                    reads=[dbuf, B("VH")],
                                    writes=[B(f"PS1q{t2 * 2 + hf}")], annot="f1d")
                for t2 in range(2):
                    ti = pair * 2 + t2
                    g = s * TPS + ch * 4 + ti
                    ev("scalar", lambda e, t2=t2, ti=ti: e.activation(
                        SC2[:, :], PS1[:, t2 * 1024:(t2 + 1) * 1024], AF.Identity,
                        scale=SML[:, 232 + ti * 3:233 + ti * 3]),
                        reads=[B(f"PS1q{t2 * 2}"), B(f"PS1q{t2 * 2 + 1}")],
                        sreads=[B(f"TC{ti}_2")], writes=[B("SC2")], annot="f1ev")
                    _ln_res_store(g, last=last)

    if upto >= 1:
        attn_block(0)
    if upto >= 2:
        ffn0_block()
    if upto >= 3:
        attn_block(1)
    if upto >= 4:
        ffn1_block(last=True)
    elif DBG["point"] is None:
        for g in range(NTT):
            ev("scalar", lambda e, g=g: e.copy(QSTG[:, 0:1024], XRES[:, g * D:(g + 1) * D]),
               reads=[B("XRES")], writes=[B("QSTG")])
            dma("sync", AP(hb["out"], g * 128 * D, [[D, 128], [1, D]]), QSTG[:, 0:1024],
                reads=[B("QSTG")], annot="dbgout")

    P.finalize()
    totals = P.totals()

    with nc.Block() as blk:
        @blk.tensor
        def _(t):
            P.replay_one(t, "tensor", sems)

        @blk.vector
        def _(v):
            P.replay_one(v, "vector", sems)

        @blk.scalar
        def _(sc):
            P.replay_one(sc, "scalar", sems)

        @blk.gpsimd
        def _(gp):
            P.replay_one(gp, "gpsimd", sems)
            for i in range(NDMALANES):
                n = totals[("gpsimd", i)]
                if n:
                    gp.wait_ge(sems[("gpsimd", i)], 16 * n)

        @blk.sync
        def _(sy):
            P.replay_one(sy, "sync", sems)
            for i in range(NDMALANES):
                n = totals[("sync", i)]
                if n:
                    sy.wait_ge(sems[("sync", i)], 16 * n)

    for cm in reversed(ctx):
        cm.__exit__(None, None, None)
    return nc


def kernel(**inputs):
    I = {k: np.asarray(v) for k, v in inputs.items()}
    x = np.asarray(I["L0_attn_x"], np.float32)
    lay = host_layouts(I)
    consts = host_constants()
    nc = build_program(upto=4)
    in_maps = []
    for c in range(NCORES):
        m = {"x_in": np.ascontiguousarray(
            x[c * SEQ_PC:(c + 1) * SEQ_PC].reshape(SEQ_PC * S, D))}
        for k, v in consts.items():
            m[k] = v
        for k in HOST_LAYOUT_2D:
            m[k] = lay[k]
        in_maps.append(m)
    res = run_bass_kernel_spmd(nc, in_maps, core_ids=list(range(NCORES)))
    outs = [np.asarray(res.results[c]["out"]).reshape(SEQ_PC, S, D)
            for c in range(NCORES)]
    return np.concatenate(outs, axis=0).astype(np.float32)
